# version 15
# baseline (speedup 1.0000x reference)
import numpy as np
import ml_dtypes
import concourse.bass as bass
import concourse.mybir as mybir
from concourse.bass_utils import run_bass_kernel_spmd

F32 = mybir.dt.float32
BF16 = mybir.dt.bfloat16
AF = mybir.ActivationFunctionType
ALU = mybir.AluOpType

E = 2048
NTOK = 2048
H = 16
NEC = 16
OWN = 1024
HID = 8192
EPS = 1e-5
SM_SHIFT = -16.0
INV_SQRT_D = float(1.0 / np.sqrt(128.0))

CHUNK_IDS = {0: [15, 13, 11, 9, 6, 4, 2, 0], 1: [14, 12, 10, 8, 7, 5, 3, 1]}
KMAX = [2048, 1792, 1536, 1280, 1024, 768, 512, 256]
NSLOT = 8
FFN_GSZ = 4
NFFN_G = HID // 128 // FFN_GSZ


def build_nc(upto=6, dbg=False, ctx_halves=2, p2_variant=0, kv_delay=0, ffn_lim=None):
    nc = bass.Bass("TRN2", target_bir_lowering=False)

    xo_d = nc.dram_tensor("xo", [128, NEC * OWN], F32, kind="ExternalInput").ap()
    xc_d = nc.dram_tensor("xc", [2 * E, 1024], F32, kind="ExternalInput").ap()
    hq_d = nc.dram_tensor("hq", [H * 128, E], F32, kind="ExternalInput").ap()
    hkv_d = nc.dram_tensor("hkv", [2 * 128, E], F32, kind="ExternalInput").ap()
    bq_d = nc.dram_tensor("bq", [128, H], F32, kind="ExternalInput").ap()
    bkv_d = nc.dram_tensor("bkv", [128, 2], F32, kind="ExternalInput").ap()
    hfc_d = nc.dram_tensor("hfc", [4 * E, 512], F32, kind="ExternalInput").ap()
    bfc_d = nc.dram_tensor("bfc", [128, NEC], F32, kind="ExternalInput").ap()
    hg_d = nc.dram_tensor("hg", [HID, E], F32, kind="ExternalInput").ap()
    hu_d = nc.dram_tensor("hu", [HID, E], F32, kind="ExternalInput").ap()
    hd_d = nc.dram_tensor("hd", [E, HID], F32, kind="ExternalInput").ap()
    mask_d = nc.dram_tensor("masks", [128, NSLOT * 2 * 512], BF16, kind="ExternalInput").ap()
    idf_d = nc.dram_tensor("identf", [128, 128], F32, kind="ExternalInput").ap()
    idb_d = nc.dram_tensor("identb", [128, 128], BF16, kind="ExternalInput").ap()
    onc_d = nc.dram_tensor("onesc", [128, 1], BF16, kind="ExternalInput").ap()
    onr_d = nc.dram_tensor("onesr", [1, 128], F32, kind="ExternalInput").ap()
    cst_d = nc.dram_tensor("csts", [128, 2], F32, kind="ExternalInput").ap()
    out_d = nc.dram_tensor("outT", [128, NEC * OWN], F32, kind="ExternalOutput").ap()

    dbg_names = []
    dbg_d = {}
    if dbg:
        for nm, shp, dt in [("dbg_xn", [128, NEC * OWN], BF16),
                            ("dbg_kt", [128, NTOK], BF16),
                            ("dbg_v", [128, NTOK], BF16),
                            ("dbg_qt", [128, 4 * OWN], BF16),
                            ("dbg_ao", [128, H * OWN], BF16),
                            ("dbg_ps", [128, 4096], F32),
                            ("dbg_lrow", [1, 2048], F32)]:
            dbg_d[nm] = nc.dram_tensor(nm, shp, dt, kind="ExternalOutput").ap()
            dbg_names.append(nm)

    import contextlib
    with contextlib.ExitStack() as _st:
        def SB(name, shape, dt):
            return _st.enter_context(nc.sbuf_tensor(name, shape, dt))
        acc = SB("acc", [128, NEC * OWN], F32)
        xn_own = SB("xn_own", [128, NEC * OWN], BF16)
        ao = SB("ao", [128, H * OWN], BF16)
        qt = SB("qt", [128, 4 * OWN], BF16)
        kt = SB("kt", [128, NTOK], BF16)
        vv = SB("vv", [128, NTOK], BF16)
        xbsq = SB("xbsq", [128, 1024], BF16)
        f32a = SB("f32a", [128, 1024], F32)
        f32b = SB("f32b", [128, 1024], F32)
        lrow = SB("lrow", [1, 2048], F32)
        rcp = SB("rcp", [128, 1024], F32)
        pt = SB("pt", [128, 2 * 512], BF16)
        gated = SB("gated", [128, FFN_GSZ * OWN], BF16)
        mk = SB("mk", [128, 2 * 512], BF16)
        wq_b = SB("wq_b", [128, 2 * E], BF16)
        wg_b = SB("wg_b", [128, 2 * E], BF16)
        wu_b = SB("wu_b", [128, 2 * E], BF16)
        wd_b = SB("wd_b", [128, 4 * 512], BF16)
        idf = SB("idf", [128, 128], F32)
        idb = SB("idb", [128, 128], BF16)
        cst_t = SB("cst_t", [128, 2], F32)
        ones_c = SB("ones_c2", [128, 1], BF16)
        ones_r = SB("ones_r2", [1, 128], F32)
        bq_t = SB("bq_t", [128, H], F32)
        bkv_t = SB("bkv_t", [128, 2], F32)
        bfc_t = SB("bfc_t", [128, NEC], F32)
        ps = _st.enter_context(nc.psum_tensor("ps", [128, 4096], F32))
        hY = _st.enter_context(nc.semaphore("sY"))
        hG = _st.enter_context(nc.semaphore("sG"))
        hT = _st.enter_context(nc.semaphore("sT"))
        hV = _st.enter_context(nc.semaphore("sV"))
        hS = _st.enter_context(nc.semaphore("sS"))
        hO = _st.enter_context(nc.semaphore("sO"))
        block = _st.enter_context(nc.Block())
        cnt = {"Y": 0, "G": 0, "T": 0, "V": 0, "S": 0, "O": 0}
        sem = {"Y": hY, "G": hG, "T": hT, "V": hV, "S": hS, "O": hO}

        def fin(ins, nm, k=1):
            ins.then_inc(sem[nm], k)
            cnt[nm] += k
            return cnt[nm]

        qt_ap = qt[:]
        ao_ap = ao[:]
        ps_ap = ps[:]

        def ap3(base_ap, off, d1, d2):
            return bass.AP(base_ap.tensor, base_ap.offset + off,
                           [list(base_ap.ap[0]), list(d1), list(d2)])

        @block.sync
        def _(y):
            fin(y.dma_start(out=acc[:], in_=xo_d), "Y", 16)
            fin(y.dma_start(out=idf[:], in_=idf_d), "Y", 16)
            fin(y.dma_start(out=idb[:], in_=idb_d), "Y", 16)
            fin(y.dma_start(out=ones_c[:], in_=onc_d), "Y", 16)
            fin(y.dma_start(out=ones_r[:], in_=onr_d), "Y", 16)
            fin(y.dma_start(out=cst_t[:], in_=cst_d), "Y", 16)
            fin(y.dma_start(out=bq_t[:], in_=bq_d), "Y", 16)
            fin(y.dma_start(out=bkv_t[:], in_=bkv_d), "Y", 16)
            fin(y.dma_start(out=bfc_t[:], in_=bfc_d), "Y", 16)
        y_init = cnt["Y"]
        shift_t = cst_t[:, 0:1]
        eps_t = cst_t[:, 1:2]

        @block.gpsimd
        def _(g):
            fin(g.dma_start(out=wg_b[:, 0:E], in_=hkv_d[0:128, :]), "G", 16)
            fin(g.dma_start(out=wu_b[:, 0:E], in_=hkv_d[128:256, :]), "G", 16)
        g_wkv = cnt["G"]

        def emit_ln_tail(extra_bcast_wait_v=0):
            t_stats = cnt["T"]

            @block.scalar
            def _(s):
                s.wait_ge(hT, t_stats)
                s.activation(lrow[0:1, 0:1024], ps[0:1, 0:1024], AF.Copy, scale=1.0 / E)
                s.activation(lrow[0:1, 1024:2048], ps[0:1, 1024:2048], AF.Copy, scale=1.0 / E)
                fin(s.activation(ps[0:1, 2048:3072], ps[0:1, 0:1024], AF.Square,
                                 scale=1.0 / E), "S")
            s_f1 = cnt["S"]

            @block.vector
            def _(v):
                v.wait_ge(hS, s_f1)
                fin(v.tensor_sub(lrow[0:1, 1024:2048], lrow[0:1, 1024:2048], ps[0:1, 2048:3072]), "V")
                fin(v.memset(gated[0:1, 0:512], 0.0), "V")
            v_f1 = cnt["V"]

            @block.scalar
            def _(s):
                s.wait_ge(hV, v_f1)
                s.wait_ge(hS, s_f1)
                fin(s.activation(lrow[0:1, 1024:2048], lrow[0:1, 1024:2048], AF.Ln,
                                 bias=eps_t[0:1, 0:1]), "S")
                s.wait_ge(hS, cnt["S"])
                fin(s.activation(lrow[0:1, 1024:2048], lrow[0:1, 1024:2048], AF.Exp, scale=-0.5), "S")
            s_f2 = cnt["S"]

            @block.tensor
            def _(t):
                t.wait_ge(hS, s_f2)
                if extra_bcast_wait_v:
                    t.wait_ge(hV, extra_bcast_wait_v)
                for hh in range(2):
                    t.matmul(ps[:, 2048 + hh * 512:2560 + hh * 512], ones_r[:],
                             lrow[0:1, 1024 + hh * 512:1536 + hh * 512], start=True, stop=True)
                t.matmul(ps[:, 3072:3584], ones_r[:], lrow[0:1, 0:512], start=True, stop=True)
                fin(t.matmul(ps[:, 3584:4096], ones_r[:], lrow[0:1, 512:1024],
                             start=True, stop=True), "T")
            emit_ln_tail.f1 = (s_f1, v_f1)
            return cnt["T"]

        s_xn_own = 0
        if upto >= 1:
            t_base = cnt["T"]

            @block.scalar
            def _(s):
                for i in range(2 * NEC):
                    ec, hh = divmod(i, 2)
                    sb_ = xbsq if i % 2 == 0 else mk
                    if i == 0:
                        s.wait_ge(hY, y_init)
                    if i >= 2:
                        s.wait_ge(hT, t_base + i - 1)
                    xa = acc[:, ec * OWN + hh * 512: ec * OWN + hh * 512 + 512]
                    s.activation(sb_[:, 0:512], xa, AF.Copy)
                    fin(s.activation(sb_[:, 512:1024], xa, AF.Square), "S")
            s_casts = cnt["S"]

            @block.tensor
            def _(t):
                for i in range(2 * NEC):
                    ec, hh = divmod(i, 2)
                    sb_ = xbsq if i % 2 == 0 else mk
                    t.wait_ge(hS, s_casts - 2 * NEC + i + 1)
                    t.matmul(ps[0:1, hh * 512:hh * 512 + 512], ones_c[:], sb_[:, 0:512],
                             start=(ec == 0), stop=(ec == NEC - 1))
                    fin(t.matmul(ps[0:1, 1024 + hh * 512:1536 + hh * 512], ones_c[:],
                                 sb_[:, 512:1024], start=(ec == 0), stop=(ec == NEC - 1)), "T")
            assert cnt["T"] == t_base + 2 * NEC

            t_bc_own = emit_ln_tail()
            s_f1_own, v_f1_own = emit_ln_tail.f1
            t_stats_own = t_bc_own - 5 if False else t_base + 2 * NEC
            sc_base = cnt["S"]

            @block.vector
            def _(v):
                for ec in range(NEC):
                    tmp = f32a if ec % 2 == 0 else f32b
                    if ec == 0:
                        v.wait_ge(hT, t_bc_own)
                    if ec >= 2:
                        v.wait_ge(hS, sc_base + ec - 1)
                    fin(v.tensor_sub(tmp[:], acc[:, ec * OWN:(ec + 1) * OWN], ps[:, 3072:4096]), "V")
                    v.wait_ge(hV, cnt["V"])
                    fin(v.tensor_mul(tmp[:], tmp[:], ps[:, 2048:3072]), "V")
                fin(v.memset(gated[0:1, 512:1024], 0.0), "V")
            v_apps = cnt["V"]

            @block.scalar
            def _(s):
                for ec in range(NEC):
                    tmp = f32a if ec % 2 == 0 else f32b
                    s.wait_ge(hV, min(v_apps - 33 + 2 * ec + 3, v_apps))
                    fin(s.activation(xn_own[:, ec * OWN:(ec + 1) * OWN], tmp[:], AF.Copy), "S")
            assert cnt["S"] == sc_base + NEC
            s_xn_own = cnt["S"]

        s_v_done = 0
        t_kv_last = 0
        t_vtr_prev = 0
        if upto >= 2:
            for half in range(ctx_halves):
                def src_of(i):
                    ec, hh = divmod(i, 2)
                    r0 = (half * NEC + ec) * 128
                    return xc_d[r0:r0 + 128, hh * 512:hh * 512 + 512]

                def a_buf(i):
                    t_ = f32a if (i % 4) < 2 else f32b
                    return t_[:, (i % 2) * 512:(i % 2) * 512 + 512]

                ld_a = {}
                sa_base = cnt["S"]

                @block.sync
                def _(y):
                    for i in range(2 * NEC):
                        if i < 4:
                            if half == 0:
                                y.wait_ge(hS, s_xn_own)
                            else:
                                y.wait_ge(hV, app31_prev)
                                y.wait_ge(hT, t_vtr_prev)
                        else:
                            y.wait_ge(hS, sa_base + i - 4 + 1)
                        fin(y.dma_start(out=a_buf(i), in_=src_of(i)), "Y", 16)
                        ld_a[i] = cnt["Y"]

                ta_base = cnt["T"]

                @block.scalar
                def _(s):
                    for i in range(2 * NEC):
                        sb_ = xbsq if i % 2 == 0 else mk
                        if i >= 2:
                            s.wait_ge(hT, ta_base + i - 1)
                        s.wait_ge(hY, ld_a[i])
                        if half == 0 and i == 0:
                            s.wait_ge(hT, t_stats_own)
                        if half == 1 and i == 0:
                            s.wait_ge(hS, s_v_done)
                        xa = a_buf(i)
                        s.activation(sb_[:, 0:512], xa, AF.Copy)
                        fin(s.activation(sb_[:, 512:1024], xa, AF.Square), "S")
                assert cnt["S"] == sa_base + 2 * NEC

                @block.tensor
                def _(t):
                    for i in range(2 * NEC):
                        ec, hh = divmod(i, 2)
                        sb_ = xbsq if i % 2 == 0 else mk
                        t.wait_ge(hS, sa_base + i + 1)
                        if i == 0 and half == 0:
                            t.wait_ge(hS, s_f1_own)
                            t.wait_ge(hV, v_f1_own)
                        if i == 0 and half == 1:
                            t.wait_ge(hS, s_v_done)
                        t.matmul(ps[0:1, hh * 512:hh * 512 + 512], ones_c[:], sb_[:, 0:512],
                                 start=(ec == 0), stop=(ec == NEC - 1))
                        fin(t.matmul(ps[0:1, 1024 + hh * 512:1536 + hh * 512], ones_c[:],
                                     sb_[:, 512:1024], start=(ec == 0), stop=(ec == NEC - 1)), "T")
                assert cnt["T"] == ta_base + 2 * NEC
                cast_done = {i: sa_base + i + 1 for i in range(2 * NEC)}

                t_bc = emit_ln_tail(extra_bcast_wait_v=(v_apps if half == 0 else app31_prev))

                ld_b = {}
                vb_base = cnt["V"]
                sb_base = cnt["S"]
                tb_base = cnt["T"]

                @block.sync
                def _(y):
                    for i in range(2 * NEC):
                        if i < 4:
                            y.wait_ge(hS, cast_done[28 + i])
                        else:
                            y.wait_ge(hV, vb_base + 2 * (i - 4) + 1)
                        fin(y.dma_start(out=a_buf(i), in_=src_of(i)), "Y", 16)
                        ld_b[i] = cnt["Y"]

                @block.vector
                def _(v):
                    for i in range(2 * NEC):
                        ec, hh = divmod(i, 2)
                        rtmp = rcp[:, (i % 2) * 512:(i % 2) * 512 + 512]
                        v.wait_ge(hY, ld_b[i])
                        if i == 0:
                            v.wait_ge(hT, t_bc)
                        if i < 2 and half == 0:
                            v.wait_ge(hS, s_xn_own)
                        if i >= 2:
                            v.wait_ge(hS, sb_base + i - 2 + 1)
                        fin(v.tensor_sub(rtmp, a_buf(i), ps[:, 3072 + hh * 512:3584 + hh * 512]), "V")
                        v.wait_ge(hV, cnt["V"])
                        fin(v.tensor_mul(rtmp, rtmp, ps[:, 2048 + hh * 512:2560 + hh * 512]), "V")
                    fin(v.memset(gated[0:1, 1024:1536], 0.0), "V")
                assert cnt["V"] == vb_base + 4 * NEC + 1

                @block.scalar
                def _(s):
                    for i in range(2 * NEC):
                        rtmp = rcp[:, (i % 2) * 512:(i % 2) * 512 + 512]
                        xnb = xbsq[:, (i % 2) * 512:(i % 2) * 512 + 512]
                        s.wait_ge(hV, min(vb_base + 2 * i + 3, vb_base + 4 * NEC + 1))
                        if i >= 2:
                            s.wait_ge(hT, tb_base + i - 2 + 1)
                        fin(s.activation(xnb, rtmp, AF.Copy), "S")
                assert cnt["S"] == sb_base + 2 * NEC

                @block.tensor
                def _(t):
                    for i in range(2 * NEC):
                        ec, hh = divmod(i, 2)
                        xnb = xbsq[:, (i % 2) * 512:(i % 2) * 512 + 512]
                        t.wait_ge(hS, sb_base + i + 1)
                        if i == 0 and half == 0:
                            t.wait_ge(hG, g_wkv)
                        t.matmul(ps[:, hh * 512:hh * 512 + 512], wg_b[:, ec * 128:ec * 128 + 128],
                                 xnb, start=(ec == 0), stop=(ec == NEC - 1))
                        fin(t.matmul(ps[:, 1024 + hh * 512:1536 + hh * 512],
                                     wu_b[:, ec * 128:ec * 128 + 128], xnb,
                                     start=(ec == 0), stop=(ec == NEC - 1)), "T")
                assert cnt["T"] == tb_base + 2 * NEC
                app_b_v = {i: vb_base + 2 * i + 1 for i in range(2 * NEC)}
                app31_prev = app_b_v[2 * NEC - 1]
                t_kv_last = cnt["T"]

                @block.scalar
                def _(s):
                    s.wait_ge(hT, t_kv_last)
                    s.activation(kt[:, half * 1024:half * 1024 + 1024], ps[:, 0:1024], AF.Identity,
                                 bias=bkv_t[:, 0:1])
                    fin(s.activation(f32a[:], ps[:, 1024:2048], AF.Identity, bias=bkv_t[:, 1:2]), "S")
                s_kv_ev = cnt["S"]

                @block.tensor
                def _(t):
                    t.wait_ge(hS, s_kv_ev)
                    t.wait_ge(hY, y_init)
                    for i8 in range(8):
                        fin(t.transpose(ps[:, 1024 + i8 * 128:1152 + i8 * 128],
                                        f32a[:, i8 * 128:i8 * 128 + 128], idf[:]), "T")
                t_vtr = cnt["T"]
                t_vtr_prev = t_vtr

                @block.scalar
                def _(s):
                    s.wait_ge(hT, t_vtr)
                    fin(s.activation(vv[:, half * 1024:half * 1024 + 1024], ps[:, 1024:2048],
                                     AF.Copy), "S")
                s_v_done = cnt["S"]

        t_attn_end = 0
        s_ao_last = 0
        s_qev_last = 0
        v_aomul = {31: 0}
        if upto >= 3:
            t_qmm = {}
            s_qev = {}
            t_qmm_done = {}
            ao_cast = {}
            s_rcp = {}
            v_aomul = {}
            t_mk_war = t_kv_last
            last_q_use_prev_g = 0
            s_exp_last_prev = 0
            t_avrs_prev = {}
            t_avrs_last = 0
            t_bc = {}
            for sg in range(32):
                g, si = divmod(sg, NSLOT)
                ncnk = KMAX[si] // 128
                avo = 1024 + (sg % 2) * 512

                if si == 0:
                    tq_base = cnt["T"]
                    sq_base = cnt["S"]
                    g_wq = {}

                    @block.gpsimd
                    def _(gq):
                        for hi in range(4):
                            h = 4 * g + hi
                            if h >= 2:
                                if hi < 2:
                                    gq.wait_ge(hT, t_qmm_done[h - 2])
                                else:
                                    gq.wait_ge(hT, tq_base + 2 * (hi - 2) + 2)
                            fin(gq.dma_start(out=wq_b[:, (h % 2) * E:(h % 2 + 1) * E],
                                             in_=hq_d[h * 128:(h + 1) * 128, :]), "G", 16)
                            g_wq[h] = cnt["G"]

                    @block.tensor
                    def _(t):
                        for hi in range(4):
                            h = 4 * g + hi
                            for hf in range(2):
                                qb = ps[:, 2560 + hf * 512:3072 + hf * 512]
                                t.wait_ge(hG, g_wq[h])
                                if h == 0 and hf == 0:
                                    t.wait_ge(hS, s_xn_own)
                                if h >= 1:
                                    if hi == 0:
                                        t.wait_ge(hS, s_qev[(h - 1, hf)])
                                    else:
                                        t.wait_ge(hS, sq_base + 2 * (hi - 1) + hf + 1)
                                ins = None
                                for ec in range(NEC):
                                    ins = t.matmul(
                                        qb, wq_b[:, (h % 2) * E + ec * 128:(h % 2) * E + ec * 128 + 128],
                                        xn_own[:, ec * OWN + hf * 512:ec * OWN + hf * 512 + 512],
                                        start=(ec == 0), stop=(ec == NEC - 1))
                                fin(ins, "T")
                                t_qmm[(h, hf)] = cnt["T"]
                            t_qmm_done[h] = t_qmm[(h, 1)]
                    assert cnt["T"] == tq_base + 8

                    @block.scalar
                    def _(s):
                        for hi in range(4):
                            h = 4 * g + hi
                            for hf in range(2):
                                qb = ps[:, 2560 + hf * 512:3072 + hf * 512]
                                s.wait_ge(hT, t_qmm[(h, hf)])
                                if g > 0 and hi == 0 and hf == 0:
                                    s.wait_ge(hT, last_q_use_prev_g)
                                fin(s.activation(
                                    qt[:, hi * OWN + hf * 512:hi * OWN + hf * 512 + 512],
                                    qb, AF.Identity, bias=bq_t[:, h:h + 1]), "S")
                                s_qev[(h, hf)] = cnt["S"]
                    assert cnt["S"] == sq_base + 8
                    s_qev_last = cnt["S"]

                @block.sync
                def _(y):
                    y.wait_ge(hT, t_mk_war)
                    fin(y.dma_start(out=mk[:], in_=mask_d[:, si * 1024:si * 1024 + 1024]),
                        "Y", 16)
                y_mk = cnt["Y"]

                s_base = cnt["S"]
                npre = 2 if sg >= 1 else 0
                pred_rs_ln = s_base + 1
                pred_rcp = s_base + npre

                def pred_exp(c):
                    return s_base + npre + c + 1

                t_smm = {}
                t_avrs = {}

                @block.tensor
                def _(t):
                    if sg >= 1:
                        t.wait_ge(hS, pred_rs_ln)
                        fin(t.matmul(ps[:, 2048:2560], ones_r[:], lrow[0:1, 0:512],
                                     start=True, stop=True), "T")
                        t_bc[sg - 1] = cnt["T"]

                    def emit_smm(c):
                        tail = c >= ncnk - 2
                        if sg == 0 and c == 0:
                            t.wait_ge(hS, s_v_done)
                        if si == 0 and c == 0:
                            t.wait_ge(hS, s_qev[(4 * g + 3, 1)])
                        if c >= 2:
                            t.wait_ge(hS, pred_exp(c - 2))
                        elif c == 0:
                            t.wait_ge(hS, s_exp_last_prev)
                        ins = t.matmul(ps[:, (c % 2) * 512:(c % 2) * 512 + 512],
                                       kt[:, c * 128:c * 128 + 128],
                                       ap3(qt_ap, si * 128, [1024, 4], [1, 128]),
                                       start=True, stop=not tail)
                        if tail:
                            ti = c - (ncnk - 2)
                            t.wait_ge(hY, y_mk)
                            ins = t.matmul(ps[:, (c % 2) * 512:(c % 2) * 512 + 512], idb[:],
                                           mk[:, ti * 512:ti * 512 + 512],
                                           start=False, stop=True)
                        fin(ins, "T")
                        t_smm[c] = cnt["T"]

                    def emit_avrs(c):
                        pts = pt[:, (c % 2) * 512:(c % 2) * 512 + 512]
                        t.wait_ge(hS, pred_exp(c))
                        if c == 0:
                            if sg >= 2:
                                t.wait_ge(hV, v_aomul[sg - 2])
                            if sg >= 1:
                                t.wait_ge(hS, pred_rcp)
                        t.matmul(ps[:, avo:avo + 512], vv[:, c * 128:c * 128 + 128], pts,
                                 start=(c == 0), stop=(c == ncnk - 1))
                        fin(t.matmul(ps[0:1, 2048:2560], ones_c[:], pts,
                                     start=(c == 0), stop=(c == ncnk - 1)), "T")
                        t_avrs[c] = cnt["T"]

                    emit_smm(0)
                    for c in range(1, ncnk):
                        emit_smm(c)
                        emit_avrs(c - 1)
                    emit_avrs(ncnk - 1)
                t_mk_war = t_smm[ncnk - 1]

                @block.scalar
                def _(s):
                    if sg >= 1:
                        s.wait_ge(hT, t_avrs_prev[sg - 1])
                        fin(s.activation(lrow[0:1, 0:512], ps[0:1, 2048:2560], AF.Ln), "S")
                        s.wait_ge(hT, t_bc[sg - 1])
                        if sg >= 3:
                            s.wait_ge(hV, v_aomul[sg - 3])
                        fin(s.activation(rcp[:, ((sg - 1) % 2) * 512:((sg - 1) % 2) * 512 + 512],
                                         ps[:, 2048:2560], AF.Exp, scale=-1.0), "S")
                        s_rcp[sg - 1] = cnt["S"]
                    for c in range(ncnk):
                        s.wait_ge(hT, t_smm[c])
                        if c >= 2:
                            s.wait_ge(hT, t_avrs[c - 2])
                        elif sg >= 1:
                            s.wait_ge(hT, t_avrs_prev[sg - 1])
                        fin(s.activation(pt[:, (c % 2) * 512:(c % 2) * 512 + 512],
                                         ps[:, (c % 2) * 512:(c % 2) * 512 + 512], AF.Exp,
                                         bias=shift_t[:], scale=INV_SQRT_D), "S")
                assert cnt["S"] == s_base + npre + ncnk, (sg, cnt["S"], s_base, npre, ncnk)
                s_exp_last_prev = cnt["S"]
                if sg >= 1:
                    assert s_rcp[sg - 1] == pred_rcp

                if sg >= 1:
                    pg, psi = divmod(sg - 1, NSLOT)

                    @block.vector
                    def _(v):
                        v.wait_ge(hS, s_rcp[sg - 1])
                        fin(v.tensor_mul(
                            ap3(ao_ap, 4 * pg * OWN + psi * 128, [OWN, 4], [1, 128]),
                            ap3(ps_ap, 1024 + ((sg - 1) % 2) * 512, [128, 4], [1, 128]),
                            ap3(rcp[:], ((sg - 1) % 2) * 512, [128, 4], [1, 128])), "V")
                    v_aomul[sg - 1] = cnt["V"]

                t_avrs_prev[sg] = t_avrs[ncnk - 1]
                t_avrs_last = t_avrs[ncnk - 1]
                if si == NSLOT - 1:
                    last_q_use_prev_g = cnt["T"]

            @block.scalar
            def _(s):
                s.wait_ge(hT, t_avrs_prev[31])
                fin(s.activation(lrow[0:1, 0:512], ps[0:1, 2048:2560], AF.Ln), "S")
            s_ep1 = cnt["S"]

            @block.tensor
            def _(t):
                t.wait_ge(hS, s_ep1)
                fin(t.matmul(ps[:, 2048:2560], ones_r[:], lrow[0:1, 0:512],
                             start=True, stop=True), "T")
            t_ep = cnt["T"]

            @block.scalar
            def _(s):
                s.wait_ge(hT, t_ep)
                fin(s.activation(rcp[:, 512:1024], ps[:, 2048:2560], AF.Exp, scale=-1.0), "S")
            s_rcp[31] = cnt["S"]

            @block.vector
            def _(v):
                v.wait_ge(hS, s_rcp[31])
                fin(v.tensor_mul(
                    ap3(ao_ap, 3 * 4 * OWN + 7 * 128, [OWN, 4], [1, 128]),
                    ap3(ps_ap, 1024 + 512, [128, 4], [1, 128]),
                    ap3(rcp[:], 512, [128, 4], [1, 128])), "V")
            v_aomul[31] = cnt["V"]
            t_attn_end = cnt["T"]
            s_ao_last = cnt["S"]

        v_fc_last = 0
        if upto >= 4:
            fc_mm = {}
            v_fc = {}
            fc_bufs = [pt, mk, xbsq]
            for cb in range(4):
                for ec in range(NEC):
                    pr = (cb * NEC + ec) // 2
                    buf = fc_bufs[pr % 3]
                    if ec % 2 == 0:
                        @block.gpsimd
                        def _(gq):
                            if cb == 0 and pr < 3:
                                gq.wait_ge(hT, t_attn_end)
                            if pr >= 3:
                                gq.wait_ge(hT, fc_mm[pr - 3])
                            r0 = (cb * NEC + ec) * 128
                            src2 = bass.AP(hfc_d.tensor, hfc_d.offset + r0 * 512,
                                           [[512, 128], [128 * 512, 2], [1, 512]])
                            fin(gq.dma_start(out=buf[:], in_=src2), "G", 16)
                        g_fc = cnt["G"]

                    @block.tensor
                    def _(t):
                        t.wait_ge(hG, g_fc)
                        if ec == 0:
                            if cb == 0:
                                t.wait_ge(hV, v_aomul[31])
                                t.nop(cycle_cnt=16384)
                            else:
                                t.wait_ge(hV, v_fc[cb - 1])
                        ins = None
                        off = (ec % 2) * 512
                        for ct in range(4):
                            for hf in range(2):
                                ins = t.matmul(ps[:, ct * 1024 + hf * 512:ct * 1024 + hf * 512 + 512],
                                               buf[:, off + ct * 128:off + ct * 128 + 128],
                                               ao[:, ec * OWN + hf * 512:ec * OWN + hf * 512 + 512],
                                               start=(ec == 0), stop=(ec == NEC - 1))
                        fin(ins, "T")
                    fc_mm[pr] = cnt["T"]

                @block.vector
                def _(v):
                    v.wait_ge(hT, fc_mm[cb * 8 + 7])
                    ins = None
                    for ct in range(4):
                        col = cb * 4 + ct
                        for hf in range(2):
                            sl = slice(col * OWN + hf * 512, col * OWN + hf * 512 + 512)
                            ins = v.scalar_tensor_tensor(
                                acc[:, sl], ps[:, ct * 1024 + hf * 512:ct * 1024 + hf * 512 + 512],
                                bfc_t[:, col:col + 1], acc[:, sl], ALU.add, ALU.add)
                    fin(ins, "V")
                v_fc[cb] = cnt["V"]
            v_fc_last = cnt["V"]

        v_z_last = 0
        if upto >= 5:
            t_up = {}
            t_z = {}
            g_wgu = {}
            s_tanh_prev = 0
            v_stt_prev = 0

            def emit_gu_loads(gi, tg_base_pred, ks=None):
                @block.gpsimd
                def _(gq):
                    for k in (ks if ks is not None else range(FFN_GSZ)):
                        ht = gi * FFN_GSZ + k
                        par = ht % 2
                        if ht <= 1:
                            gq.wait_ge(hT, t_kv_last)
                        elif k < 2:
                            gq.wait_ge(hT, t_up[ht - 2])
                        else:
                            gq.wait_ge(hT, tg_base_pred + (k - 2) + 1)
                        fin(gq.dma_start(out=wg_b[:, par * E:(par + 1) * E],
                                         in_=hg_d[ht * 128:(ht + 1) * 128, :]), "G", 16)
                        fin(gq.dma_start(out=wu_b[:, par * E:(par + 1) * E],
                                         in_=hu_d[ht * 128:(ht + 1) * 128, :]), "G", 16)
                        g_wgu[ht] = cnt["G"]

            for gi in range(ffn_lim if ffn_lim else NFFN_G):
                tg_base = cnt["T"]
                sg_base = cnt["S"]
                vg_base = cnt["V"]
                if gi == 0:
                    emit_gu_loads(0, tg_base)

                for k in range(FFN_GSZ):
                    ht = gi * FFN_GSZ + k
                    par = ht % 2

                    @block.tensor
                    def _(t):
                        t.wait_ge(hG, g_wgu[ht])
                        if ht == 0:
                            t.wait_ge(hV, v_fc_last)
                        if k == 0:
                            if gi >= 1:
                                t.wait_ge(hS, s_tanh_prev)
                                t.wait_ge(hV, v_stt_prev)
                        else:
                            t.wait_ge(hS, sg_base + k)
                            t.wait_ge(hV, vg_base + 2 * k)
                        ins = None
                        for hf in range(2):
                            for ec in range(NEC):
                                t.matmul(ps[:, hf * 512:hf * 512 + 512],
                                         wg_b[:, par * E + ec * 128:par * E + ec * 128 + 128],
                                         xn_own[:, ec * OWN + hf * 512:ec * OWN + hf * 512 + 512],
                                         start=(ec == 0), stop=(ec == NEC - 1))
                        for hf in range(2):
                            for ec in range(NEC):
                                ins = t.matmul(ps[:, 1024 + hf * 512:1536 + hf * 512],
                                               wu_b[:, par * E + ec * 128:par * E + ec * 128 + 128],
                                               xn_own[:, ec * OWN + hf * 512:ec * OWN + hf * 512 + 512],
                                               start=(ec == 0), stop=(ec == NEC - 1))
                        fin(ins, "T")
                    t_up[ht] = cnt["T"]
                assert cnt["T"] == tg_base + FFN_GSZ

                tanh_m = {0: sg_base + 1, 1: sg_base + 2, 2: sg_base + 3, 3: sg_base + 5}
                cast_m = {0: sg_base + 4, 1: sg_base + 6, 2: sg_base + 7, 3: sg_base + 8}

                def emit_tanh(s, k):
                    ht = gi * FFN_GSZ + k
                    s.wait_ge(hT, t_up[ht])
                    if k >= 1:
                        s.wait_ge(hV, vg_base + 2 * k)
                    elif gi >= 1:
                        s.wait_ge(hV, v_stt_prev)
                    fin(s.activation(f32b[:], ps[:, 0:1024], AF.Tanh, scale=0.5), "S")

                def emit_cast(s, k):
                    gtmp = f32a if k % 2 == 0 else rcp
                    s.wait_ge(hV, min(vg_base + 2 * k + 3, vg_base + 2 * FFN_GSZ + 1))
                    if gi >= 1:
                        s.wait_ge(hT, t_z[(gi - 1, NEC - 1)])
                    fin(s.activation(gated[:, k * OWN:(k + 1) * OWN], gtmp[:], AF.Copy), "S")

                @block.scalar
                def _(s):
                    emit_tanh(s, 0)
                    emit_tanh(s, 1)
                    emit_tanh(s, 2)
                    emit_cast(s, 0)
                    emit_tanh(s, 3)
                    emit_cast(s, 1)
                    emit_cast(s, 2)
                    emit_cast(s, 3)
                assert cnt["S"] == sg_base + 8
                for k in range(4):
                    assert tanh_m[k] <= sg_base + 8 and cast_m[k] <= sg_base + 8

                @block.vector
                def _(v):
                    for k in range(FFN_GSZ):
                        gtmp = f32a if k % 2 == 0 else rcp
                        v.wait_ge(hS, tanh_m[k])
                        if k >= 2:
                            v.wait_ge(hS, cast_m[k - 2])
                        elif gi >= 1:
                            v.wait_ge(hS, s_cast_prev[k])
                        elif gi == 0:
                            v.wait_ge(hS, s_xn_own)
                        fin(v.scalar_tensor_tensor(gtmp[:], f32b[:], 1.0, ps[:, 0:1024],
                                                    ALU.add, ALU.mult), "V")
                        v.wait_ge(hV, cnt["V"])
                        fin(v.scalar_tensor_tensor(gtmp[:], ps[:, 1024:2048], 0.5, gtmp[:],
                                                   ALU.mult, ALU.mult), "V")
                    fin(v.memset(kt[0:1, 0:512], 0.0), "V")
                assert cnt["V"] == vg_base + 2 * FFN_GSZ + 1
                v_stt_prev = cnt["V"] - 1
                s_tanh_prev = tanh_m[3]
                s_cast_prev = {0: cast_m[2], 1: cast_m[3]}
                s_gated_last = cast_m[3]

                vd_base = cnt["V"]
                tg_base_next_pred = cnt["T"] + (NEC - 0)
                g_wd = {}
                for dt in range(NEC):
                    par = dt % 4
                    zoff = 2048 + (dt % 2) * 1024

                    if dt % 2 == 0:
                        @block.gpsimd
                        def _(gq):
                            if (gi, dt) > (0, 3):
                                if dt >= 4:
                                    gq.wait_ge(hT, t_z[(gi, dt - 3)])
                                else:
                                    gq.wait_ge(hT, t_z[(gi - 1, NEC - 4 + dt + 1)])
                            src2 = bass.AP(hd_d.tensor, hd_d.offset + (dt * 128) * HID + gi * 512,
                                           [[HID, 128], [128 * HID, 2], [1, 512]])
                            fin(gq.dma_start(out=wd_b[:, par * 512:par * 512 + 1024],
                                             in_=src2), "G", 16)
                            g_wd[dt] = cnt["G"]
                            g_wd[dt + 1] = cnt["G"]
                    if gi + 1 < (ffn_lim if ffn_lim else NFFN_G) and dt == 12:
                        emit_gu_loads(gi + 1, tg_base_next_pred, ks=[0, 1])
                    if gi + 1 < (ffn_lim if ffn_lim else NFFN_G) and dt == 14:
                        emit_gu_loads(gi + 1, tg_base_next_pred, ks=[2, 3])

                    @block.tensor
                    def _(t):
                        t.wait_ge(hG, g_wd[dt])
                        if (gi, dt) > (0, 1):
                            if dt >= 2:
                                t.wait_ge(hV, vd_base + dt - 1)
                            else:
                                t.wait_ge(hV, v_z_prev[NEC - 2 + dt])
                        ins = None
                        for k in range(FFN_GSZ):
                            if dt <= 1:
                                t.wait_ge(hS, cast_m[k])
                            for hf in range(2):
                                ins = t.matmul(ps[:, zoff + hf * 512:zoff + hf * 512 + 512],
                                               wd_b[:, par * 512 + k * 128:par * 512 + k * 128 + 128],
                                               gated[:, k * OWN + hf * 512:k * OWN + hf * 512 + 512],
                                               start=(k == 0), stop=(k == FFN_GSZ - 1))
                        fin(ins, "T")
                    t_z[(gi, dt)] = cnt["T"]

                @block.vector
                def _(v):
                    for dt in range(NEC):
                        par = dt % 2
                        zoff = 2048 + par * 1024
                        v.wait_ge(hT, t_z[(gi, dt)])
                        ins = None
                        for hf in range(2):
                            sl = slice(dt * OWN + hf * 512, dt * OWN + hf * 512 + 512)
                            ins = v.tensor_add(acc[:, sl], acc[:, sl],
                                               ps[:, zoff + hf * 512:zoff + hf * 512 + 512])
                        fin(ins, "V")
                assert cnt["V"] == vd_base + NEC
                v_z_prev = {dt: vd_base + dt + 1 for dt in range(NEC)}
            v_z_last = cnt["V"]

        @block.vector
        def _(v):
            v.wait_ge(hT, cnt["T"])
            fin(v.tensor_copy(f32a[:], f32b[:]), "V")
        v_fin = cnt["V"]

        @block.scalar
        def _(s):
            s.wait_ge(hV, v_fin)
            s.activation(f32b[:], f32a[:], AF.Copy)
            fin(s.activation(lrow[0:1, 0:1], lrow[0:1, 0:1], AF.Copy), "S")
        s_final = cnt["S"]

        if dbg and upto < 4:
            @block.vector
            def _(v):
                v.wait_ge(hS, cnt["S"])
                v.wait_ge(hT, cnt["T"])
                fin(v.tensor_copy(acc[:, 0:4096], ps[:, 0:4096]), "V")

            @block.sync
            def _(y):
                y.wait_ge(hV, cnt["V"])
                fin(y.dma_start(out=dbg_d["dbg_ps"], in_=acc[:, 0:4096]), "O", 16)

        @block.sync
        def _(y):
            y.wait_ge(hV, cnt["V"])
            y.wait_ge(hS, cnt["S"])
            y.wait_ge(hT, cnt["T"])
            y.wait_ge(hS, s_final)
            y.nop(cycle_cnt=6000)
            y.nop(cycle_cnt=6000)
            fin(y.dma_start(out=out_d, in_=acc[:]), "O", 16)
            if dbg:
                fin(y.dma_start(out=dbg_d["dbg_xn"], in_=xn_own[:]), "O", 16)
                fin(y.dma_start(out=dbg_d["dbg_lrow"], in_=lrow[:]), "O", 16)
                fin(y.dma_start(out=dbg_d["dbg_kt"], in_=kt[:]), "O", 16)
                fin(y.dma_start(out=dbg_d["dbg_v"], in_=vv[:]), "O", 16)
                fin(y.dma_start(out=dbg_d["dbg_qt"], in_=qt[:]), "O", 16)
                fin(y.dma_start(out=dbg_d["dbg_ao"], in_=ao[:]), "O", 16)
            y.wait_ge(hO, cnt["O"])

    return nc, dbg_names, cnt



def _prep_shared(inputs):
    wqkv = np.asarray(inputs["L0_attn_w_qkv"], dtype=np.float32)
    bqkv = np.asarray(inputs["L0_attn_b_qkv"], dtype=np.float32)
    wfc = np.asarray(inputs["L0_attn_w_fc"], dtype=np.float32)
    bfc = np.asarray(inputs["L0_attn_b_fc"], dtype=np.float32)
    wg = np.asarray(inputs["L0_ffn_w_gate"], dtype=np.float32)
    wu = np.asarray(inputs["L0_ffn_w_up"], dtype=np.float32)
    wd = np.asarray(inputs["L0_ffn_w_down"], dtype=np.float32)

    wq = wqkv[:, :E]
    hq = np.ascontiguousarray(
        wq.reshape(NEC, 128, H, 128).transpose(2, 1, 0, 3).reshape(H * 128, E))
    hkv = np.ascontiguousarray(
        wqkv[:, E:E + 256].reshape(NEC, 128, 2, 128).transpose(2, 1, 0, 3).reshape(256, E))
    bq = np.ascontiguousarray(bqkv[:E].reshape(H, 128).T)
    bkv = np.ascontiguousarray(bqkv[E:E + 256].reshape(2, 128).T)
    hfc = np.ascontiguousarray(
        wfc.reshape(NEC, 128, 4, 512).transpose(2, 0, 1, 3).reshape(4 * E, 512))
    bfc_t = np.ascontiguousarray(bfc.reshape(NEC, 128).T)
    hg = np.ascontiguousarray(
        wg.reshape(64, 128, NEC, 128).transpose(0, 3, 2, 1).reshape(HID, E))
    hu = np.ascontiguousarray(
        wu.reshape(64, 128, NEC, 128).transpose(0, 3, 2, 1).reshape(HID, E))
    hd = np.ascontiguousarray(
        wd.reshape(NEC, 128, 64, 128).transpose(0, 3, 2, 1).reshape(E, HID))
    identf = np.eye(128, dtype=np.float32)
    identb = np.eye(128, dtype=ml_dtypes.bfloat16)
    onesc = np.ones((128, 1), dtype=ml_dtypes.bfloat16)
    onesr = np.ones((1, 128), dtype=np.float32)
    csts = np.zeros((128, 2), dtype=np.float32)
    csts[:, 0] = SM_SHIFT
    csts[:, 1] = EPS
    return dict(hq=hq, hkv=hkv, bq=bq, bkv=bkv, hfc=hfc, bfc=bfc_t,
                hg=hg, hu=hu, hd=hd, identf=identf, identb=identb,
                onesc=onesc, onesr=onesr, csts=csts)


def _core_rows(j):
    return np.concatenate([np.arange(c * 128, (c + 1) * 128) for c in CHUNK_IDS[j]])


def _prep_core(x, b, j):
    rows = _core_rows(j)
    xb = np.asarray(x[b], dtype=np.float32)
    xo = np.ascontiguousarray(
        xb[rows].T.reshape(NEC, 128, OWN).transpose(1, 0, 2).reshape(128, NEC * OWN))
    xc = np.ascontiguousarray(
        xb.reshape(2, 1024, NEC, 128).transpose(0, 2, 3, 1).reshape(2 * E, 1024))
    q0 = np.array([CHUNK_IDS[j][s] * 128 for s in range(NSLOT)])
    m = np.zeros((128, NSLOT, 2, 4, 128), dtype=np.float32)
    for s in range(NSLOT):
        nc_ = KMAX[s] // 128
        for ti in range(2):
            c = nc_ - 2 + ti
            k_idx = c * 128 + np.arange(128)[:, None]
            qq = q0[s] + np.arange(128)[None, :]
            keep = (k_idx <= qq)
            m[:, s, ti, :, :] = np.where(keep, 0.0, -1e9)[:, None, :]
    masks = np.ascontiguousarray(m.reshape(128, NSLOT * 2 * 512).astype(ml_dtypes.bfloat16))
    return dict(xo=xo, xc=xc, masks=masks)


_BUILD_CACHE = {}


def _get_nc():
    key = "full"
    if key not in _BUILD_CACHE:
        nc, names, _cnt = build_nc(upto=6, dbg=False)
        _BUILD_CACHE[key] = (nc, names)
    return _BUILD_CACHE[key]


def kernel(**inputs):
    x = np.asarray(inputs["L0_attn_x"], dtype=np.float32)
    shared = _prep_shared(inputs)
    nc, _ = _get_nc()
    in_maps = []
    for c in range(8):
        b, j = divmod(c, 2)
        m = dict(shared)
        m.update(_prep_core(x, b, j))
        in_maps.append(m)
    res = run_bass_kernel_spmd(nc, in_maps, core_ids=list(range(8)))
    out = np.empty((4, 2048, 2048), dtype=np.float32)
    for c in range(8):
        b, j = divmod(c, 2)
        o = res.results[c]["outT"]
        o = o.reshape(128, NEC, OWN).transpose(1, 0, 2).reshape(E, OWN)
        rows = _core_rows(j)
        out[b, rows, :] = o.T
    return out


# version 16
# speedup vs baseline: 1.3076x; 1.3076x over previous
import numpy as np
import ml_dtypes
import concourse.bass as bass
import concourse.mybir as mybir
from concourse.bass_utils import run_bass_kernel_spmd

F32 = mybir.dt.float32
BF16 = mybir.dt.bfloat16
AF = mybir.ActivationFunctionType
ALU = mybir.AluOpType

E = 2048
NTOK = 2048
H = 16
NEC = 16
OWN = 1024
HID = 8192
EPS = 1e-5
SM_SHIFT = -16.0
INV_SQRT_D = float(1.0 / np.sqrt(128.0))

CHUNK_IDS = {0: [15, 13, 11, 9, 6, 4, 2, 0], 1: [14, 12, 10, 8, 7, 5, 3, 1]}
KMAX = [2048, 1792, 1536, 1280, 1024, 768, 512, 256]
NSLOT = 8
FFN_GSZ = 4
NFFN_G = HID // 128 // FFN_GSZ


def build_nc(upto=6, dbg=False, ctx_halves=2, p2_variant=0, kv_delay=0, ffn_lim=None):
    nc = bass.Bass("TRN2", target_bir_lowering=False)

    xo_d = nc.dram_tensor("xo", [128, NEC * OWN], F32, kind="ExternalInput").ap()
    xc_d = nc.dram_tensor("xc", [2 * E, 1024], F32, kind="ExternalInput").ap()
    hq_d = nc.dram_tensor("hq", [H * 128, E], F32, kind="ExternalInput").ap()
    hkv_d = nc.dram_tensor("hkv", [2 * 128, E], F32, kind="ExternalInput").ap()
    bq_d = nc.dram_tensor("bq", [128, H], F32, kind="ExternalInput").ap()
    bkv_d = nc.dram_tensor("bkv", [128, 2], F32, kind="ExternalInput").ap()
    hfc_d = nc.dram_tensor("hfc", [4 * E, 512], F32, kind="ExternalInput").ap()
    bfc_d = nc.dram_tensor("bfc", [128, NEC], F32, kind="ExternalInput").ap()
    hg_d = nc.dram_tensor("hg", [HID, E], F32, kind="ExternalInput").ap()
    hu_d = nc.dram_tensor("hu", [HID, E], F32, kind="ExternalInput").ap()
    hd_d = nc.dram_tensor("hd", [E, HID], F32, kind="ExternalInput").ap()
    mask_d = nc.dram_tensor("masks", [128, NSLOT * 2 * 512], BF16, kind="ExternalInput").ap()
    idf_d = nc.dram_tensor("identf", [128, 128], F32, kind="ExternalInput").ap()
    idb_d = nc.dram_tensor("identb", [128, 128], BF16, kind="ExternalInput").ap()
    onc_d = nc.dram_tensor("onesc", [128, 1], BF16, kind="ExternalInput").ap()
    onr_d = nc.dram_tensor("onesr", [1, 128], F32, kind="ExternalInput").ap()
    cst_d = nc.dram_tensor("csts", [128, 2], F32, kind="ExternalInput").ap()
    out_d = nc.dram_tensor("outT", [128, NEC * OWN], F32, kind="ExternalOutput").ap()

    dbg_names = []
    dbg_d = {}
    if dbg:
        for nm, shp, dt in [("dbg_xn", [128, NEC * OWN], BF16),
                            ("dbg_kt", [128, NTOK], BF16),
                            ("dbg_v", [128, NTOK], BF16),
                            ("dbg_qt", [128, 4 * OWN], BF16),
                            ("dbg_ao", [128, H * OWN], BF16),
                            ("dbg_ps", [128, 4096], F32),
                            ("dbg_lrow", [1, 2048], F32)]:
            dbg_d[nm] = nc.dram_tensor(nm, shp, dt, kind="ExternalOutput").ap()
            dbg_names.append(nm)

    import contextlib
    with contextlib.ExitStack() as _st:
        def SB(name, shape, dt):
            return _st.enter_context(nc.sbuf_tensor(name, shape, dt))
        acc = SB("acc", [128, NEC * OWN], F32)
        xn_own = SB("xn_own", [128, NEC * OWN], BF16)
        ao = SB("ao", [128, H * OWN], BF16)
        qt = SB("qt", [128, 4 * OWN], BF16)
        kt = SB("kt", [128, NTOK], BF16)
        vv = SB("vv", [128, NTOK], BF16)
        xbsq = SB("xbsq", [128, 1024], BF16)
        f32a = SB("f32a", [128, 1024], F32)
        f32b = SB("f32b", [128, 1024], F32)
        lrow = SB("lrow", [1, 2048], F32)
        rcp = SB("rcp", [128, 1024], F32)
        pt = SB("pt", [128, 2 * 512], BF16)
        gated = SB("gated", [128, FFN_GSZ * OWN], BF16)
        mk = SB("mk", [128, 2 * 512], BF16)
        wq_b = SB("wq_b", [128, 2 * E], BF16)
        wg_b = SB("wg_b", [128, 2 * E], BF16)
        wu_b = SB("wu_b", [128, 2 * E], BF16)
        wd_b = SB("wd_b", [128, 4 * 512], BF16)
        idf = SB("idf", [128, 128], F32)
        idb = SB("idb", [128, 128], BF16)
        cst_t = SB("cst_t", [128, 2], F32)
        ones_c = SB("ones_c2", [128, 1], BF16)
        ones_r = SB("ones_r2", [1, 128], F32)
        bq_t = SB("bq_t", [128, H], F32)
        bkv_t = SB("bkv_t", [128, 2], F32)
        bfc_t = SB("bfc_t", [128, NEC], F32)
        ps = _st.enter_context(nc.psum_tensor("ps", [128, 4096], F32))
        hY = _st.enter_context(nc.semaphore("sY"))
        hG = _st.enter_context(nc.semaphore("sG"))
        hT = _st.enter_context(nc.semaphore("sT"))
        hV = _st.enter_context(nc.semaphore("sV"))
        hS = _st.enter_context(nc.semaphore("sS"))
        hO = _st.enter_context(nc.semaphore("sO"))
        block = _st.enter_context(nc.Block())
        cnt = {"Y": 0, "G": 0, "T": 0, "V": 0, "S": 0, "O": 0}
        sem = {"Y": hY, "G": hG, "T": hT, "V": hV, "S": hS, "O": hO}

        def fin(ins, nm, k=1):
            ins.then_inc(sem[nm], k)
            cnt[nm] += k
            return cnt[nm]

        qt_ap = qt[:]
        ao_ap = ao[:]
        ps_ap = ps[:]

        def ap3(base_ap, off, d1, d2):
            return bass.AP(base_ap.tensor, base_ap.offset + off,
                           [list(base_ap.ap[0]), list(d1), list(d2)])

        @block.sync
        def _(y):
            fin(y.dma_start(out=acc[:], in_=xo_d), "Y", 16)
            fin(y.dma_start(out=idf[:], in_=idf_d), "Y", 16)
            fin(y.dma_start(out=idb[:], in_=idb_d), "Y", 16)
            fin(y.dma_start(out=ones_c[:], in_=onc_d), "Y", 16)
            fin(y.dma_start(out=ones_r[:], in_=onr_d), "Y", 16)
            fin(y.dma_start(out=cst_t[:], in_=cst_d), "Y", 16)
            fin(y.dma_start(out=bq_t[:], in_=bq_d), "Y", 16)
            fin(y.dma_start(out=bkv_t[:], in_=bkv_d), "Y", 16)
            fin(y.dma_start(out=bfc_t[:], in_=bfc_d), "Y", 16)
        y_init = cnt["Y"]
        shift_t = cst_t[:, 0:1]
        eps_t = cst_t[:, 1:2]

        @block.gpsimd
        def _(g):
            fin(g.dma_start(out=wg_b[:, 0:E], in_=hkv_d[0:128, :]), "G", 16)
            fin(g.dma_start(out=wu_b[:, 0:E], in_=hkv_d[128:256, :]), "G", 16)
        g_wkv = cnt["G"]

        def emit_ln_tail(extra_bcast_wait_v=0):
            t_stats = cnt["T"]

            @block.scalar
            def _(s):
                s.wait_ge(hT, t_stats)
                s.activation(lrow[0:1, 0:1024], ps[0:1, 0:1024], AF.Copy, scale=1.0 / E)
                s.activation(lrow[0:1, 1024:2048], ps[0:1, 1024:2048], AF.Copy, scale=1.0 / E)
                fin(s.activation(ps[0:1, 2048:3072], ps[0:1, 0:1024], AF.Square,
                                 scale=1.0 / E), "S")
            s_f1 = cnt["S"]

            @block.vector
            def _(v):
                v.wait_ge(hS, s_f1)
                fin(v.tensor_sub(lrow[0:1, 1024:2048], lrow[0:1, 1024:2048], ps[0:1, 2048:3072]), "V")
                fin(v.memset(gated[0:1, 0:512], 0.0), "V")
            v_f1 = cnt["V"]

            @block.scalar
            def _(s):
                s.wait_ge(hV, v_f1)
                s.wait_ge(hS, s_f1)
                fin(s.activation(lrow[0:1, 1024:2048], lrow[0:1, 1024:2048], AF.Ln,
                                 bias=eps_t[0:1, 0:1]), "S")
                s.wait_ge(hS, cnt["S"])
                fin(s.activation(lrow[0:1, 1024:2048], lrow[0:1, 1024:2048], AF.Exp, scale=-0.5), "S")
            s_f2 = cnt["S"]

            @block.tensor
            def _(t):
                t.wait_ge(hS, s_f2)
                if extra_bcast_wait_v:
                    t.wait_ge(hV, extra_bcast_wait_v)
                for hh in range(2):
                    t.matmul(ps[:, 2048 + hh * 512:2560 + hh * 512], ones_r[:],
                             lrow[0:1, 1024 + hh * 512:1536 + hh * 512], start=True, stop=True)
                t.matmul(ps[:, 3072:3584], ones_r[:], lrow[0:1, 0:512], start=True, stop=True)
                fin(t.matmul(ps[:, 3584:4096], ones_r[:], lrow[0:1, 512:1024],
                             start=True, stop=True), "T")
            emit_ln_tail.f1 = (s_f1, v_f1)
            return cnt["T"]

        s_xn_own = 0
        if upto >= 1:
            t_base = cnt["T"]

            @block.scalar
            def _(s):
                for i in range(2 * NEC):
                    ec, hh = divmod(i, 2)
                    sb_ = xbsq if i % 2 == 0 else mk
                    if i == 0:
                        s.wait_ge(hY, y_init)
                    if i >= 2:
                        s.wait_ge(hT, t_base + i - 1)
                    xa = acc[:, ec * OWN + hh * 512: ec * OWN + hh * 512 + 512]
                    s.activation(sb_[:, 0:512], xa, AF.Copy)
                    fin(s.activation(sb_[:, 512:1024], xa, AF.Square), "S")
            s_casts = cnt["S"]

            @block.tensor
            def _(t):
                for i in range(2 * NEC):
                    ec, hh = divmod(i, 2)
                    sb_ = xbsq if i % 2 == 0 else mk
                    t.wait_ge(hS, s_casts - 2 * NEC + i + 1)
                    t.matmul(ps[0:1, hh * 512:hh * 512 + 512], ones_c[:], sb_[:, 0:512],
                             start=(ec == 0), stop=(ec == NEC - 1))
                    fin(t.matmul(ps[0:1, 1024 + hh * 512:1536 + hh * 512], ones_c[:],
                                 sb_[:, 512:1024], start=(ec == 0), stop=(ec == NEC - 1)), "T")
            assert cnt["T"] == t_base + 2 * NEC

            t_bc_own = emit_ln_tail()
            s_f1_own, v_f1_own = emit_ln_tail.f1
            t_stats_own = t_bc_own - 5 if False else t_base + 2 * NEC
            sc_base = cnt["S"]

            @block.vector
            def _(v):
                for ec in range(NEC):
                    tmp = f32a if ec % 2 == 0 else f32b
                    if ec == 0:
                        v.wait_ge(hT, t_bc_own)
                    if ec >= 2:
                        v.wait_ge(hS, sc_base + ec - 1)
                    fin(v.tensor_sub(tmp[:], acc[:, ec * OWN:(ec + 1) * OWN], ps[:, 3072:4096]), "V")
                    v.wait_ge(hV, cnt["V"])
                    fin(v.tensor_mul(tmp[:], tmp[:], ps[:, 2048:3072]), "V")
                fin(v.memset(gated[0:1, 512:1024], 0.0), "V")
            v_apps = cnt["V"]

            @block.scalar
            def _(s):
                for ec in range(NEC):
                    tmp = f32a if ec % 2 == 0 else f32b
                    s.wait_ge(hV, min(v_apps - 33 + 2 * ec + 3, v_apps))
                    fin(s.activation(xn_own[:, ec * OWN:(ec + 1) * OWN], tmp[:], AF.Copy), "S")
            assert cnt["S"] == sc_base + NEC
            s_xn_own = cnt["S"]

        s_v_done = 0
        t_kv_last = 0
        t_vtr_prev = 0
        if upto >= 2:
            for half in range(ctx_halves):
                def src_of(i):
                    ec, hh = divmod(i, 2)
                    r0 = (half * NEC + ec) * 128
                    return xc_d[r0:r0 + 128, hh * 512:hh * 512 + 512]

                def a_buf(i):
                    t_ = f32a if (i % 4) < 2 else f32b
                    return t_[:, (i % 2) * 512:(i % 2) * 512 + 512]

                ld_a = {}
                sa_base = cnt["S"]

                @block.sync
                def _(y):
                    for i in range(2 * NEC):
                        if i < 4:
                            if half == 0:
                                y.wait_ge(hS, s_xn_own)
                            else:
                                y.wait_ge(hV, app31_prev)
                                y.wait_ge(hT, t_vtr_prev)
                        else:
                            y.wait_ge(hS, sa_base + i - 4 + 1)
                        fin(y.dma_start(out=a_buf(i), in_=src_of(i)), "Y", 16)
                        ld_a[i] = cnt["Y"]

                ta_base = cnt["T"]

                @block.scalar
                def _(s):
                    for i in range(2 * NEC):
                        sb_ = xbsq if i % 2 == 0 else mk
                        if i >= 2:
                            s.wait_ge(hT, ta_base + i - 1)
                        s.wait_ge(hY, ld_a[i])
                        if half == 0 and i == 0:
                            s.wait_ge(hT, t_stats_own)
                        if half == 1 and i == 0:
                            s.wait_ge(hS, s_v_done)
                        xa = a_buf(i)
                        s.activation(sb_[:, 0:512], xa, AF.Copy)
                        fin(s.activation(sb_[:, 512:1024], xa, AF.Square), "S")
                assert cnt["S"] == sa_base + 2 * NEC

                @block.tensor
                def _(t):
                    for i in range(2 * NEC):
                        ec, hh = divmod(i, 2)
                        sb_ = xbsq if i % 2 == 0 else mk
                        t.wait_ge(hS, sa_base + i + 1)
                        if i == 0 and half == 0:
                            t.wait_ge(hS, s_f1_own)
                            t.wait_ge(hV, v_f1_own)
                        if i == 0 and half == 1:
                            t.wait_ge(hS, s_v_done)
                        t.matmul(ps[0:1, hh * 512:hh * 512 + 512], ones_c[:], sb_[:, 0:512],
                                 start=(ec == 0), stop=(ec == NEC - 1))
                        fin(t.matmul(ps[0:1, 1024 + hh * 512:1536 + hh * 512], ones_c[:],
                                     sb_[:, 512:1024], start=(ec == 0), stop=(ec == NEC - 1)), "T")
                assert cnt["T"] == ta_base + 2 * NEC
                cast_done = {i: sa_base + i + 1 for i in range(2 * NEC)}

                t_bc = emit_ln_tail(extra_bcast_wait_v=(v_apps if half == 0 else app31_prev))

                ld_b = {}
                vb_base = cnt["V"]
                sb_base = cnt["S"]
                tb_base = cnt["T"]

                @block.sync
                def _(y):
                    for i in range(2 * NEC):
                        if i < 4:
                            y.wait_ge(hS, cast_done[28 + i])
                        else:
                            y.wait_ge(hV, vb_base + 2 * (i - 4) + 1)
                        fin(y.dma_start(out=a_buf(i), in_=src_of(i)), "Y", 16)
                        ld_b[i] = cnt["Y"]

                @block.vector
                def _(v):
                    for i in range(2 * NEC):
                        ec, hh = divmod(i, 2)
                        rtmp = rcp[:, (i % 2) * 512:(i % 2) * 512 + 512]
                        v.wait_ge(hY, ld_b[i])
                        if i == 0:
                            v.wait_ge(hT, t_bc)
                        if i < 2 and half == 0:
                            v.wait_ge(hS, s_xn_own)
                        if i >= 2:
                            v.wait_ge(hS, sb_base + i - 2 + 1)
                        fin(v.tensor_sub(rtmp, a_buf(i), ps[:, 3072 + hh * 512:3584 + hh * 512]), "V")
                        v.wait_ge(hV, cnt["V"])
                        fin(v.tensor_mul(rtmp, rtmp, ps[:, 2048 + hh * 512:2560 + hh * 512]), "V")
                    fin(v.memset(gated[0:1, 1024:1536], 0.0), "V")
                assert cnt["V"] == vb_base + 4 * NEC + 1

                @block.scalar
                def _(s):
                    for i in range(2 * NEC):
                        rtmp = rcp[:, (i % 2) * 512:(i % 2) * 512 + 512]
                        xnb = xbsq[:, (i % 2) * 512:(i % 2) * 512 + 512]
                        s.wait_ge(hV, min(vb_base + 2 * i + 3, vb_base + 4 * NEC + 1))
                        if i >= 2:
                            s.wait_ge(hT, tb_base + i - 2 + 1)
                        fin(s.activation(xnb, rtmp, AF.Copy), "S")
                assert cnt["S"] == sb_base + 2 * NEC

                @block.tensor
                def _(t):
                    for i in range(2 * NEC):
                        ec, hh = divmod(i, 2)
                        xnb = xbsq[:, (i % 2) * 512:(i % 2) * 512 + 512]
                        t.wait_ge(hS, sb_base + i + 1)
                        if i == 0 and half == 0:
                            t.wait_ge(hG, g_wkv)
                        t.matmul(ps[:, hh * 512:hh * 512 + 512], wg_b[:, ec * 128:ec * 128 + 128],
                                 xnb, start=(ec == 0), stop=(ec == NEC - 1))
                        fin(t.matmul(ps[:, 1024 + hh * 512:1536 + hh * 512],
                                     wu_b[:, ec * 128:ec * 128 + 128], xnb,
                                     start=(ec == 0), stop=(ec == NEC - 1)), "T")
                assert cnt["T"] == tb_base + 2 * NEC
                app_b_v = {i: vb_base + 2 * i + 1 for i in range(2 * NEC)}
                app31_prev = app_b_v[2 * NEC - 1]
                t_kv_last = cnt["T"]

                @block.scalar
                def _(s):
                    s.wait_ge(hT, t_kv_last)
                    s.activation(kt[:, half * 1024:half * 1024 + 1024], ps[:, 0:1024], AF.Identity,
                                 bias=bkv_t[:, 0:1])
                    fin(s.activation(f32a[:], ps[:, 1024:2048], AF.Identity, bias=bkv_t[:, 1:2]), "S")
                s_kv_ev = cnt["S"]

                @block.tensor
                def _(t):
                    t.wait_ge(hS, s_kv_ev)
                    t.wait_ge(hY, y_init)
                    for i8 in range(8):
                        fin(t.transpose(ps[:, 1024 + i8 * 128:1152 + i8 * 128],
                                        f32a[:, i8 * 128:i8 * 128 + 128], idf[:]), "T")
                t_vtr = cnt["T"]
                t_vtr_prev = t_vtr

                @block.scalar
                def _(s):
                    s.wait_ge(hT, t_vtr)
                    fin(s.activation(vv[:, half * 1024:half * 1024 + 1024], ps[:, 1024:2048],
                                     AF.Copy), "S")
                s_v_done = cnt["S"]

        t_attn_end = 0
        s_ao_last = 0
        s_qev_last = 0
        v_aomul = {31: 0}
        if upto >= 3:
            t_qmm = {}
            s_qev = {}
            t_qmm_done = {}
            ao_cast = {}
            s_rcp = {}
            v_aomul = {}
            t_mk_war = t_kv_last
            last_q_use_prev_g = 0
            s_exp_last_prev = 0
            t_avrs_prev = {}
            t_avrs_last = 0
            t_bc = {}
            for sg in range(32):
                g, si = divmod(sg, NSLOT)
                ncnk = KMAX[si] // 128
                avo = 1024 + (sg % 2) * 512

                if si == 0:
                    tq_base = cnt["T"]
                    sq_base = cnt["S"]
                    g_wq = {}

                    @block.gpsimd
                    def _(gq):
                        for hi in range(4):
                            h = 4 * g + hi
                            if h >= 2:
                                if hi < 2:
                                    gq.wait_ge(hT, t_qmm_done[h - 2])
                                else:
                                    gq.wait_ge(hT, tq_base + 2 * (hi - 2) + 2)
                            fin(gq.dma_start(out=wq_b[:, (h % 2) * E:(h % 2 + 1) * E],
                                             in_=hq_d[h * 128:(h + 1) * 128, :]), "G", 16)
                            g_wq[h] = cnt["G"]

                    @block.tensor
                    def _(t):
                        for hi in range(4):
                            h = 4 * g + hi
                            for hf in range(2):
                                qb = ps[:, 2560 + hf * 512:3072 + hf * 512]
                                t.wait_ge(hG, g_wq[h])
                                if h == 0 and hf == 0:
                                    t.wait_ge(hS, s_xn_own)
                                if h >= 1:
                                    if hi == 0:
                                        t.wait_ge(hS, s_qev[(h - 1, hf)])
                                    else:
                                        t.wait_ge(hS, sq_base + 2 * (hi - 1) + hf + 1)
                                ins = None
                                for ec in range(NEC):
                                    ins = t.matmul(
                                        qb, wq_b[:, (h % 2) * E + ec * 128:(h % 2) * E + ec * 128 + 128],
                                        xn_own[:, ec * OWN + hf * 512:ec * OWN + hf * 512 + 512],
                                        start=(ec == 0), stop=(ec == NEC - 1))
                                fin(ins, "T")
                                t_qmm[(h, hf)] = cnt["T"]
                            t_qmm_done[h] = t_qmm[(h, 1)]
                    assert cnt["T"] == tq_base + 8

                    @block.scalar
                    def _(s):
                        for hi in range(4):
                            h = 4 * g + hi
                            for hf in range(2):
                                qb = ps[:, 2560 + hf * 512:3072 + hf * 512]
                                s.wait_ge(hT, t_qmm[(h, hf)])
                                if g > 0 and hi == 0 and hf == 0:
                                    s.wait_ge(hT, last_q_use_prev_g)
                                fin(s.activation(
                                    qt[:, hi * OWN + hf * 512:hi * OWN + hf * 512 + 512],
                                    qb, AF.Identity, bias=bq_t[:, h:h + 1]), "S")
                                s_qev[(h, hf)] = cnt["S"]
                    assert cnt["S"] == sq_base + 8
                    s_qev_last = cnt["S"]

                @block.sync
                def _(y):
                    y.wait_ge(hT, t_mk_war)
                    fin(y.dma_start(out=mk[:], in_=mask_d[:, si * 1024:si * 1024 + 1024]),
                        "Y", 16)
                y_mk = cnt["Y"]

                s_base = cnt["S"]
                npre = 2 if sg >= 1 else 0
                pred_rs_ln = s_base + 1
                pred_rcp = s_base + npre

                def pred_exp(c):
                    return s_base + npre + c + 1

                t_smm = {}
                t_avrs = {}

                @block.tensor
                def _(t):
                    if sg >= 1:
                        t.wait_ge(hS, pred_rs_ln)
                        fin(t.matmul(ps[:, 2048:2560], ones_r[:], lrow[0:1, 0:512],
                                     start=True, stop=True), "T")
                        t_bc[sg - 1] = cnt["T"]

                    def emit_smm(c):
                        tail = c >= ncnk - 2
                        if sg == 0 and c == 0:
                            t.wait_ge(hS, s_v_done)
                        if si == 0 and c == 0:
                            t.wait_ge(hS, s_qev[(4 * g + 3, 1)])
                        if c >= 2:
                            t.wait_ge(hS, pred_exp(c - 2))
                        elif c == 0:
                            t.wait_ge(hS, s_exp_last_prev)
                        ins = t.matmul(ps[:, (c % 2) * 512:(c % 2) * 512 + 512],
                                       kt[:, c * 128:c * 128 + 128],
                                       ap3(qt_ap, si * 128, [1024, 4], [1, 128]),
                                       start=True, stop=not tail)
                        if tail:
                            ti = c - (ncnk - 2)
                            t.wait_ge(hY, y_mk)
                            ins = t.matmul(ps[:, (c % 2) * 512:(c % 2) * 512 + 512], idb[:],
                                           mk[:, ti * 512:ti * 512 + 512],
                                           start=False, stop=True)
                        fin(ins, "T")
                        t_smm[c] = cnt["T"]

                    def emit_avrs(c):
                        pts = pt[:, (c % 2) * 512:(c % 2) * 512 + 512]
                        t.wait_ge(hS, pred_exp(c))
                        if c == 0:
                            if sg >= 2:
                                t.wait_ge(hV, v_aomul[sg - 2])
                            if sg >= 1:
                                t.wait_ge(hS, pred_rcp)
                        t.matmul(ps[:, avo:avo + 512], vv[:, c * 128:c * 128 + 128], pts,
                                 start=(c == 0), stop=(c == ncnk - 1))
                        fin(t.matmul(ps[0:1, 2048:2560], ones_c[:], pts,
                                     start=(c == 0), stop=(c == ncnk - 1)), "T")
                        t_avrs[c] = cnt["T"]

                    emit_smm(0)
                    for c in range(1, ncnk):
                        emit_smm(c)
                        emit_avrs(c - 1)
                    emit_avrs(ncnk - 1)
                t_mk_war = t_smm[ncnk - 1]

                @block.scalar
                def _(s):
                    if sg >= 1:
                        s.wait_ge(hT, t_avrs_prev[sg - 1])
                        fin(s.activation(lrow[0:1, 0:512], ps[0:1, 2048:2560], AF.Ln), "S")
                        s.wait_ge(hT, t_bc[sg - 1])
                        if sg >= 3:
                            s.wait_ge(hV, v_aomul[sg - 3])
                        fin(s.activation(rcp[:, ((sg - 1) % 2) * 512:((sg - 1) % 2) * 512 + 512],
                                         ps[:, 2048:2560], AF.Exp, scale=-1.0), "S")
                        s_rcp[sg - 1] = cnt["S"]
                    for c in range(ncnk):
                        s.wait_ge(hT, t_smm[c])
                        if c >= 2:
                            s.wait_ge(hT, t_avrs[c - 2])
                        elif sg >= 1:
                            s.wait_ge(hT, t_avrs_prev[sg - 1])
                        fin(s.activation(pt[:, (c % 2) * 512:(c % 2) * 512 + 512],
                                         ps[:, (c % 2) * 512:(c % 2) * 512 + 512], AF.Exp,
                                         bias=shift_t[:], scale=INV_SQRT_D), "S")
                assert cnt["S"] == s_base + npre + ncnk, (sg, cnt["S"], s_base, npre, ncnk)
                s_exp_last_prev = cnt["S"]
                if sg >= 1:
                    assert s_rcp[sg - 1] == pred_rcp

                if sg >= 1:
                    pg, psi = divmod(sg - 1, NSLOT)

                    @block.vector
                    def _(v):
                        v.wait_ge(hS, s_rcp[sg - 1])
                        fin(v.tensor_mul(
                            ap3(ao_ap, 4 * pg * OWN + psi * 128, [OWN, 4], [1, 128]),
                            ap3(ps_ap, 1024 + ((sg - 1) % 2) * 512, [128, 4], [1, 128]),
                            ap3(rcp[:], ((sg - 1) % 2) * 512, [128, 4], [1, 128])), "V")
                    v_aomul[sg - 1] = cnt["V"]

                t_avrs_prev[sg] = t_avrs[ncnk - 1]
                t_avrs_last = t_avrs[ncnk - 1]
                if si == NSLOT - 1:
                    last_q_use_prev_g = cnt["T"]

            @block.scalar
            def _(s):
                s.wait_ge(hT, t_avrs_prev[31])
                fin(s.activation(lrow[0:1, 0:512], ps[0:1, 2048:2560], AF.Ln), "S")
            s_ep1 = cnt["S"]

            @block.tensor
            def _(t):
                t.wait_ge(hS, s_ep1)
                fin(t.matmul(ps[:, 2048:2560], ones_r[:], lrow[0:1, 0:512],
                             start=True, stop=True), "T")
            t_ep = cnt["T"]

            @block.scalar
            def _(s):
                s.wait_ge(hT, t_ep)
                fin(s.activation(rcp[:, 512:1024], ps[:, 2048:2560], AF.Exp, scale=-1.0), "S")
            s_rcp[31] = cnt["S"]

            @block.vector
            def _(v):
                v.wait_ge(hS, s_rcp[31])
                fin(v.tensor_mul(
                    ap3(ao_ap, 3 * 4 * OWN + 7 * 128, [OWN, 4], [1, 128]),
                    ap3(ps_ap, 1024 + 512, [128, 4], [1, 128]),
                    ap3(rcp[:], 512, [128, 4], [1, 128])), "V")
            v_aomul[31] = cnt["V"]
            t_attn_end = cnt["T"]
            s_ao_last = cnt["S"]

        v_fc_last = 0
        if upto >= 4:
            fc_mm = {}
            v_fc = {}
            fc_bufs = [pt, mk, xbsq]
            for cb in range(4):
                for ec in range(NEC):
                    pr = (cb * NEC + ec) // 2
                    buf = fc_bufs[pr % 3]
                    if ec % 2 == 0:
                        @block.gpsimd
                        def _(gq):
                            if cb == 0 and pr < 3:
                                gq.wait_ge(hT, t_attn_end)
                            if pr >= 3:
                                gq.wait_ge(hT, fc_mm[pr - 3])
                            r0 = (cb * NEC + ec) * 128
                            src2 = bass.AP(hfc_d.tensor, hfc_d.offset + r0 * 512,
                                           [[512, 128], [128 * 512, 2], [1, 512]])
                            fin(gq.dma_start(out=buf[:], in_=src2), "G", 16)
                        g_fc = cnt["G"]

                    @block.tensor
                    def _(t):
                        t.wait_ge(hG, g_fc)
                        if ec == 0:
                            if cb == 0:
                                t.wait_ge(hV, v_aomul[31])
                                t.nop(cycle_cnt=16384)
                            else:
                                t.wait_ge(hV, v_fc[cb - 1])
                        ins = None
                        off = (ec % 2) * 512
                        for ct in range(4):
                            for hf in range(2):
                                ins = t.matmul(ps[:, ct * 1024 + hf * 512:ct * 1024 + hf * 512 + 512],
                                               buf[:, off + ct * 128:off + ct * 128 + 128],
                                               ao[:, ec * OWN + hf * 512:ec * OWN + hf * 512 + 512],
                                               start=(ec == 0), stop=(ec == NEC - 1))
                        fin(ins, "T")
                    fc_mm[pr] = cnt["T"]

                @block.vector
                def _(v):
                    v.wait_ge(hT, fc_mm[cb * 8 + 7])
                    ins = None
                    for ct in range(4):
                        col = cb * 4 + ct
                        for hf in range(2):
                            sl = slice(col * OWN + hf * 512, col * OWN + hf * 512 + 512)
                            ins = v.scalar_tensor_tensor(
                                acc[:, sl], ps[:, ct * 1024 + hf * 512:ct * 1024 + hf * 512 + 512],
                                bfc_t[:, col:col + 1], acc[:, sl], ALU.add, ALU.add)
                    fin(ins, "V")
                v_fc[cb] = cnt["V"]
            v_fc_last = cnt["V"]

        v_z_last = 0
        if upto >= 5:
            t_up = {}
            t_z = {}
            g_wgu = {}
            s_tanh_prev = 0
            v_stt_prev = 0

            def emit_gu_loads(gi, tg_base_pred, ks=None):
                @block.gpsimd
                def _(gq):
                    for k in (ks if ks is not None else range(FFN_GSZ)):
                        ht = gi * FFN_GSZ + k
                        par = ht % 2
                        if ht <= 1:
                            gq.wait_ge(hT, t_kv_last)
                        elif k < 2:
                            gq.wait_ge(hT, t_up[ht - 2])
                        else:
                            gq.wait_ge(hT, tg_base_pred + (k - 2) + 1)
                        fin(gq.dma_start(out=wg_b[:, par * E:(par + 1) * E],
                                         in_=hg_d[ht * 128:(ht + 1) * 128, :]), "G", 16)
                        fin(gq.dma_start(out=wu_b[:, par * E:(par + 1) * E],
                                         in_=hu_d[ht * 128:(ht + 1) * 128, :]), "G", 16)
                        g_wgu[ht] = cnt["G"]

            for gi in range(ffn_lim if ffn_lim else NFFN_G):
                tg_base = cnt["T"]
                sg_base = cnt["S"]
                vg_base = cnt["V"]
                tanh_m = {0: sg_base + 1, 1: sg_base + 2, 2: sg_base + 3, 3: sg_base + 5}
                cast_m = {0: sg_base + 4, 1: sg_base + 6, 2: sg_base + 7, 3: sg_base + 8}
                if gi == 0:
                    emit_gu_loads(0, tg_base)

                for k in range(FFN_GSZ):
                    ht = gi * FFN_GSZ + k
                    par = ht % 2
                    boff = (k % 2) * 2048

                    @block.tensor
                    def _(t):
                        t.wait_ge(hG, g_wgu[ht])
                        if ht == 0:
                            t.wait_ge(hV, v_fc_last)
                        if k < 2:
                            if gi >= 1:
                                t.wait_ge(hS, s_tanh_prev2[k])
                                t.wait_ge(hV, v_stt_prev2[k])
                                if k == 1:
                                    t.wait_ge(hV, v_z_prev[NEC - 1])
                        else:
                            t.wait_ge(hS, tanh_m[k - 2])
                            t.wait_ge(hV, vg_base + 2 * (k - 2) + 2)
                        ins = None
                        for hf in range(2):
                            for ec in range(NEC):
                                t.matmul(ps[:, boff + hf * 512:boff + hf * 512 + 512],
                                         wg_b[:, par * E + ec * 128:par * E + ec * 128 + 128],
                                         xn_own[:, ec * OWN + hf * 512:ec * OWN + hf * 512 + 512],
                                         start=(ec == 0), stop=(ec == NEC - 1))
                        for hf in range(2):
                            for ec in range(NEC):
                                ins = t.matmul(ps[:, boff + 1024 + hf * 512:boff + 1536 + hf * 512],
                                               wu_b[:, par * E + ec * 128:par * E + ec * 128 + 128],
                                               xn_own[:, ec * OWN + hf * 512:ec * OWN + hf * 512 + 512],
                                               start=(ec == 0), stop=(ec == NEC - 1))
                        fin(ins, "T")
                    t_up[ht] = cnt["T"]
                assert cnt["T"] == tg_base + FFN_GSZ

                def emit_tanh(s, k):
                    ht = gi * FFN_GSZ + k
                    boff = (k % 2) * 2048
                    s.wait_ge(hT, t_up[ht])
                    if k >= 1:
                        s.wait_ge(hV, vg_base + 2 * k)
                    elif gi >= 1:
                        s.wait_ge(hV, v_stt_prev)
                    fin(s.activation(f32b[:], ps[:, boff:boff + 1024], AF.Tanh, scale=0.5), "S")

                def emit_cast(s, k):
                    gtmp = f32a if k % 2 == 0 else rcp
                    s.wait_ge(hV, min(vg_base + 2 * k + 3, vg_base + 2 * FFN_GSZ + 1))
                    if gi >= 1:
                        s.wait_ge(hT, t_z[(gi - 1, NEC - 1)])
                    fin(s.activation(gated[:, k * OWN:(k + 1) * OWN], gtmp[:], AF.Copy), "S")

                @block.scalar
                def _(s):
                    emit_tanh(s, 0)
                    emit_tanh(s, 1)
                    emit_tanh(s, 2)
                    emit_cast(s, 0)
                    emit_tanh(s, 3)
                    emit_cast(s, 1)
                    emit_cast(s, 2)
                    emit_cast(s, 3)
                assert cnt["S"] == sg_base + 8
                for k in range(4):
                    assert tanh_m[k] <= sg_base + 8 and cast_m[k] <= sg_base + 8

                @block.vector
                def _(v):
                    for k in range(FFN_GSZ):
                        gtmp = f32a if k % 2 == 0 else rcp
                        v.wait_ge(hS, tanh_m[k])
                        if k >= 2:
                            v.wait_ge(hS, cast_m[k - 2])
                        elif gi >= 1:
                            v.wait_ge(hS, s_cast_prev[k])
                        elif gi == 0:
                            v.wait_ge(hS, s_xn_own)
                        boff = (k % 2) * 2048
                        fin(v.scalar_tensor_tensor(gtmp[:], f32b[:], 1.0, ps[:, boff:boff + 1024],
                                                    ALU.add, ALU.mult), "V")
                        v.wait_ge(hV, cnt["V"])
                        fin(v.scalar_tensor_tensor(gtmp[:], ps[:, boff + 1024:boff + 2048], 0.5, gtmp[:],
                                                   ALU.mult, ALU.mult), "V")
                    fin(v.memset(kt[0:1, 0:512], 0.0), "V")
                assert cnt["V"] == vg_base + 2 * FFN_GSZ + 1
                v_stt_prev = cnt["V"] - 1
                s_tanh_prev2 = {0: tanh_m[2], 1: tanh_m[3]}
                v_stt_prev2 = {0: vg_base + 6, 1: vg_base + 8}
                s_cast_prev = {0: cast_m[2], 1: cast_m[3]}
                s_gated_last = cast_m[3]

                vd_base = cnt["V"]
                tg_base_next_pred = cnt["T"] + (NEC - 0)
                g_wd = {}
                for dt in range(NEC):
                    par = dt % 4
                    zoff = 2048 + (dt % 2) * 1024

                    if dt % 2 == 0:
                        @block.gpsimd
                        def _(gq):
                            if (gi, dt) > (0, 3):
                                if dt >= 4:
                                    gq.wait_ge(hT, t_z[(gi, dt - 3)])
                                else:
                                    gq.wait_ge(hT, t_z[(gi - 1, NEC - 4 + dt + 1)])
                            src2 = bass.AP(hd_d.tensor, hd_d.offset + (dt * 128) * HID + gi * 512,
                                           [[HID, 128], [128 * HID, 2], [1, 512]])
                            fin(gq.dma_start(out=wd_b[:, par * 512:par * 512 + 1024],
                                             in_=src2), "G", 16)
                            g_wd[dt] = cnt["G"]
                            g_wd[dt + 1] = cnt["G"]
                    if gi + 1 < (ffn_lim if ffn_lim else NFFN_G) and dt == 12:
                        emit_gu_loads(gi + 1, tg_base_next_pred, ks=[0, 1])
                    if gi + 1 < (ffn_lim if ffn_lim else NFFN_G) and dt == 14:
                        emit_gu_loads(gi + 1, tg_base_next_pred, ks=[2, 3])

                    @block.tensor
                    def _(t):
                        t.wait_ge(hG, g_wd[dt])
                        if (gi, dt) > (0, 1):
                            if dt >= 2:
                                t.wait_ge(hV, vd_base + dt - 1)
                            else:
                                t.wait_ge(hV, v_z_prev[NEC - 2 + dt])
                        ins = None
                        if dt <= 1:
                            t.wait_ge(hS, tanh_m[3])
                            t.wait_ge(hV, vg_base + 8)
                        for k in range(FFN_GSZ):
                            if dt <= 1:
                                t.wait_ge(hS, cast_m[k])
                            for hf in range(2):
                                ins = t.matmul(ps[:, zoff + hf * 512:zoff + hf * 512 + 512],
                                               wd_b[:, par * 512 + k * 128:par * 512 + k * 128 + 128],
                                               gated[:, k * OWN + hf * 512:k * OWN + hf * 512 + 512],
                                               start=(k == 0), stop=(k == FFN_GSZ - 1))
                        fin(ins, "T")
                    t_z[(gi, dt)] = cnt["T"]

                @block.vector
                def _(v):
                    for dt in range(NEC):
                        par = dt % 2
                        zoff = 2048 + par * 1024
                        v.wait_ge(hT, t_z[(gi, dt)])
                        ins = None
                        for hf in range(2):
                            sl = slice(dt * OWN + hf * 512, dt * OWN + hf * 512 + 512)
                            ins = v.tensor_add(acc[:, sl], acc[:, sl],
                                               ps[:, zoff + hf * 512:zoff + hf * 512 + 512])
                        fin(ins, "V")
                assert cnt["V"] == vd_base + NEC
                v_z_prev = {dt: vd_base + dt + 1 for dt in range(NEC)}
            v_z_last = cnt["V"]

        @block.vector
        def _(v):
            v.wait_ge(hT, cnt["T"])
            fin(v.tensor_copy(f32a[:], f32b[:]), "V")
        v_fin = cnt["V"]

        @block.scalar
        def _(s):
            s.wait_ge(hV, v_fin)
            s.activation(f32b[:], f32a[:], AF.Copy)
            fin(s.activation(lrow[0:1, 0:1], lrow[0:1, 0:1], AF.Copy), "S")
        s_final = cnt["S"]

        if dbg and upto < 4:
            @block.vector
            def _(v):
                v.wait_ge(hS, cnt["S"])
                v.wait_ge(hT, cnt["T"])
                fin(v.tensor_copy(acc[:, 0:4096], ps[:, 0:4096]), "V")

            @block.sync
            def _(y):
                y.wait_ge(hV, cnt["V"])
                fin(y.dma_start(out=dbg_d["dbg_ps"], in_=acc[:, 0:4096]), "O", 16)

        @block.sync
        def _(y):
            y.wait_ge(hV, cnt["V"])
            y.wait_ge(hS, cnt["S"])
            y.wait_ge(hT, cnt["T"])
            y.wait_ge(hS, s_final)
            y.nop(cycle_cnt=6000)
            y.nop(cycle_cnt=6000)
            fin(y.dma_start(out=out_d, in_=acc[:]), "O", 16)
            if dbg:
                fin(y.dma_start(out=dbg_d["dbg_xn"], in_=xn_own[:]), "O", 16)
                fin(y.dma_start(out=dbg_d["dbg_lrow"], in_=lrow[:]), "O", 16)
                fin(y.dma_start(out=dbg_d["dbg_kt"], in_=kt[:]), "O", 16)
                fin(y.dma_start(out=dbg_d["dbg_v"], in_=vv[:]), "O", 16)
                fin(y.dma_start(out=dbg_d["dbg_qt"], in_=qt[:]), "O", 16)
                fin(y.dma_start(out=dbg_d["dbg_ao"], in_=ao[:]), "O", 16)
            y.wait_ge(hO, cnt["O"])

    return nc, dbg_names, cnt



def _prep_shared(inputs):
    wqkv = np.asarray(inputs["L0_attn_w_qkv"], dtype=np.float32)
    bqkv = np.asarray(inputs["L0_attn_b_qkv"], dtype=np.float32)
    wfc = np.asarray(inputs["L0_attn_w_fc"], dtype=np.float32)
    bfc = np.asarray(inputs["L0_attn_b_fc"], dtype=np.float32)
    wg = np.asarray(inputs["L0_ffn_w_gate"], dtype=np.float32)
    wu = np.asarray(inputs["L0_ffn_w_up"], dtype=np.float32)
    wd = np.asarray(inputs["L0_ffn_w_down"], dtype=np.float32)

    wq = wqkv[:, :E]
    hq = np.ascontiguousarray(
        wq.reshape(NEC, 128, H, 128).transpose(2, 1, 0, 3).reshape(H * 128, E))
    hkv = np.ascontiguousarray(
        wqkv[:, E:E + 256].reshape(NEC, 128, 2, 128).transpose(2, 1, 0, 3).reshape(256, E))
    bq = np.ascontiguousarray(bqkv[:E].reshape(H, 128).T)
    bkv = np.ascontiguousarray(bqkv[E:E + 256].reshape(2, 128).T)
    hfc = np.ascontiguousarray(
        wfc.reshape(NEC, 128, 4, 512).transpose(2, 0, 1, 3).reshape(4 * E, 512))
    bfc_t = np.ascontiguousarray(bfc.reshape(NEC, 128).T)
    hg = np.ascontiguousarray(
        wg.reshape(64, 128, NEC, 128).transpose(0, 3, 2, 1).reshape(HID, E))
    hu = np.ascontiguousarray(
        wu.reshape(64, 128, NEC, 128).transpose(0, 3, 2, 1).reshape(HID, E))
    hd = np.ascontiguousarray(
        wd.reshape(NEC, 128, 64, 128).transpose(0, 3, 2, 1).reshape(E, HID))
    identf = np.eye(128, dtype=np.float32)
    identb = np.eye(128, dtype=ml_dtypes.bfloat16)
    onesc = np.ones((128, 1), dtype=ml_dtypes.bfloat16)
    onesr = np.ones((1, 128), dtype=np.float32)
    csts = np.zeros((128, 2), dtype=np.float32)
    csts[:, 0] = SM_SHIFT
    csts[:, 1] = EPS
    return dict(hq=hq, hkv=hkv, bq=bq, bkv=bkv, hfc=hfc, bfc=bfc_t,
                hg=hg, hu=hu, hd=hd, identf=identf, identb=identb,
                onesc=onesc, onesr=onesr, csts=csts)


def _core_rows(j):
    return np.concatenate([np.arange(c * 128, (c + 1) * 128) for c in CHUNK_IDS[j]])


def _prep_core(x, b, j):
    rows = _core_rows(j)
    xb = np.asarray(x[b], dtype=np.float32)
    xo = np.ascontiguousarray(
        xb[rows].T.reshape(NEC, 128, OWN).transpose(1, 0, 2).reshape(128, NEC * OWN))
    xc = np.ascontiguousarray(
        xb.reshape(2, 1024, NEC, 128).transpose(0, 2, 3, 1).reshape(2 * E, 1024))
    q0 = np.array([CHUNK_IDS[j][s] * 128 for s in range(NSLOT)])
    m = np.zeros((128, NSLOT, 2, 4, 128), dtype=np.float32)
    for s in range(NSLOT):
        nc_ = KMAX[s] // 128
        for ti in range(2):
            c = nc_ - 2 + ti
            k_idx = c * 128 + np.arange(128)[:, None]
            qq = q0[s] + np.arange(128)[None, :]
            keep = (k_idx <= qq)
            m[:, s, ti, :, :] = np.where(keep, 0.0, -1e9)[:, None, :]
    masks = np.ascontiguousarray(m.reshape(128, NSLOT * 2 * 512).astype(ml_dtypes.bfloat16))
    return dict(xo=xo, xc=xc, masks=masks)


_BUILD_CACHE = {}


def _get_nc():
    key = "full"
    if key not in _BUILD_CACHE:
        nc, names, _cnt = build_nc(upto=6, dbg=False)
        _BUILD_CACHE[key] = (nc, names)
    return _BUILD_CACHE[key]


def kernel(**inputs):
    x = np.asarray(inputs["L0_attn_x"], dtype=np.float32)
    shared = _prep_shared(inputs)
    nc, _ = _get_nc()
    in_maps = []
    for c in range(8):
        b, j = divmod(c, 2)
        m = dict(shared)
        m.update(_prep_core(x, b, j))
        in_maps.append(m)
    res = run_bass_kernel_spmd(nc, in_maps, core_ids=list(range(8)))
    out = np.empty((4, 2048, 2048), dtype=np.float32)
    for c in range(8):
        b, j = divmod(c, 2)
        o = res.results[c]["outT"]
        o = o.reshape(128, NEC, OWN).transpose(1, 0, 2).reshape(E, OWN)
        rows = _core_rows(j)
        out[b, rows, :] = o.T
    return out
